# version 14
# baseline (speedup 1.0000x reference)
import contextlib
import numpy as np

import concourse.bass as bass
import concourse.mybir as mybir
from concourse.bass_utils import run_bass_kernel_spmd
from concourse.alu_op_type import AluOpType as ALU

AF = mybir.ActivationFunctionType
F32 = mybir.dt.float32
BF16 = mybir.dt.bfloat16

B, T, D = 4, 4096, 1024
HEADS = 16
TOWN = 2048
SLEN = 4096
NQC = 4
NST = 32
NREG = 16
EPS = 1e-4
LN2EPS = float(np.log(2.0 * EPS))
LNEPS = 1e-5
NCH = 8

ENGINES = ("sync", "tensor", "vector", "scalar", "gpsimd")


class Sem:
    __slots__ = ("name", "n", "h")

    def __init__(self, name):
        self.name, self.n, self.h = name, 0, None


def build_nc():
    nc = bass.Bass("TRN2", target_bir_lowering=False)
    steps = {e: [] for e in ENGINES}
    sems = []

    def mksem(name):
        s = Sem(name)
        sems.append(s)
        return s

    cur_steps = [steps]

    def st(engine, fn, waits=(), inc=None, inc_n=1):
        if inc is not None:
            inc.n += inc_n
        cur_steps[0][engine].append((list(waits), fn, inc, inc_n))

    x_h = nc.dram_tensor("x", [SLEN, D], F32, kind="ExternalInput").ap()
    xT_h = nc.dram_tensor("xT", [D, TOWN], F32, kind="ExternalInput").ap()
    win_h = nc.dram_tensor("w_in", [D, HEADS * 256], F32, kind="ExternalInput").ap()
    bin_h = nc.dram_tensor("b_in_t", [128, 32], F32, kind="ExternalInput").ap()
    wout_h = nc.dram_tensor("w_out", [D, D], F32, kind="ExternalInput").ap()
    bout_h = nc.dram_tensor("b_out_t", [128, 8], F32, kind="ExternalInput").ap()
    wffn_h = nc.dram_tensor("w_ffn", [D, D], F32, kind="ExternalInput").ap()
    identb_h = nc.dram_tensor("identb", [128, 128], BF16, kind="ExternalInput").ap()
    cst_h = nc.dram_tensor("cst", [128, 8], F32, kind="ExternalInput").ap()
    y_h = nc.dram_tensor("y", [D, TOWN], F32, kind="ExternalOutput").ap()

    guards = {}

    def salloc(name, shape, dtype, side="left"):
        g = nc.sbuf_tensor(name, shape, dtype, side=side)
        t = g.__enter__()
        guards[name] = g
        return t

    def sfree(*names):
        for nm in names:
            guards.pop(nm).__exit__(None, None, None)

    def palloc(name, shape, dtype, side="left"):
        g = nc.psum_tensor(name, shape, dtype, side=side)
        t = g.__enter__()
        guards[name] = g
        return t

    cst = salloc("c_cst", [128, 8], F32)
    identb = salloc("c_identb", [128, 128], BF16)
    bin_sb = salloc("bin_sb", [128, 32], F32)
    bout_sb = salloc("bout_sb", [128, 8], F32)
    ones_col = salloc("ones_col", [128, 1], BF16)
    q25_col = salloc("q25_col", [128, 1], BF16)
    i1024_col = salloc("i1024_col", [128, 1], BF16)
    ones_rowb = salloc("ones_rowb", [1, 512], BF16)
    lnrows = salloc("lnrows", [128, 8], F32)
    sqscr = salloc("sqscr", [128, D], F32, side="right")
    xst = [salloc(f"xst{i}", [128, D], F32, side="right") for i in range(2)]

    XnT = salloc("XnT", [128, NCH * SLEN], BF16)
    xnf = [salloc(f"xnf{i}", [128, D], BF16, side="right") for i in range(2)]

    pp = palloc("pp", [128, 512], F32)
    pT = [palloc(f"pT{i}", [128, 512], BF16, side="right") for i in range(2)]
    pS = [palloc(f"pS{i}", [128, 1024], F32, side="right") for i in range(2)]
    pA = palloc("pA", [128, 512], F32, side="right")

    nc.const_aps.register(F32, LN2EPS, cst[:, 0:1])
    nc.const_aps.register(F32, LNEPS, cst[:, 1:2])
    nc.const_aps.register(F32, EPS, cst[:, 2:3])
    nc.const_aps.register(F32, -0.5, cst[:, 3:4])
    nc.const_aps.register(F32, -1.0, cst[:, 4:5])
    nc.const_aps.register(F32, 0.25, cst[:, 5:6])
    nc.const_aps.register(F32, 1.0, cst[:, 6:7])
    nc.const_aps.register(F32, 0.0, cst[:, 7:8])

    s_misc = mksem("misc")
    s_mset = mksem("mset")
    s_aug1 = mksem("aug1")
    s_xld = [mksem("xld0"), mksem("xld1")]
    s_xfree = [mksem(f"xfree{i}") for i in range(2)]
    s_ln = mksem("ln")
    s_lna = mksem("lna")
    s_pt = mksem("pt")
    s_ptf = mksem("ptf")
    s_xnt = mksem("xnt")
    s_wld = [mksem("wld0"), mksem("wld1")]
    s_wcast = mksem("wcast")
    s_wfree = [mksem(f"wfree{i}") for i in range(2)]
    s_pp = mksem("pp")
    s_ppf = mksem("ppf")
    s_prj = mksem("prj")
    s_sqt = mksem("sqt")
    s_vtf = [mksem(f"vtf{i}") for i in range(2)]
    s_sqf = [mksem(f"sqf{i}") for i in range(2)]
    s_k2dma = [mksem("k2dma0"), mksem("k2dma1")]
    s_hb = [mksem(f"hb{i}") for i in range(2)]
    s_hbf = [mksem(f"hbf{i}") for i in range(2)]
    s_peS = mksem("peS")
    s_c1 = mksem("c1")
    s_c2 = mksem("c2")
    s_c3 = mksem("c3")
    s_c4 = mksem("c4")
    s_wcons = mksem("wcons")
    s_e1 = mksem("e1")
    s_e0 = mksem("e0")
    s_rsd = mksem("rsd")
    s_rn = mksem("rn")
    s_bc = mksem("bc")
    s_epi = mksem("epi")
    s_hdma = mksem("hdma")
    s_wo = mksem("wo")
    s_sg = mksem("sg")
    s_xt = mksem("xt")
    s_op = mksem("op")
    s_x1 = mksem("x1")
    s_st1 = mksem("st1")
    s_str = mksem("str")
    s_sta = mksem("sta")
    s_xbf = mksem("xbf")
    s_ab = mksem("ab")
    s_abf = mksem("abf")
    s_x1n = mksem("x1n")
    s_fp = mksem("fp")
    s_x2 = mksem("x2")
    s_store = mksem("store")

    st("sync", lambda e: e.dma_start(out=cst[:], in_=cst_h), inc=s_misc, inc_n=16)
    st("sync", lambda e: e.dma_start(out=identb[:], in_=identb_h), inc=s_misc, inc_n=16)
    st("sync", lambda e: e.dma_start(out=bin_sb[:], in_=bin_h), inc=s_misc, inc_n=16)
    st("sync", lambda e: e.dma_start(out=bout_sb[:], in_=bout_h), inc=s_misc, inc_n=16)
    SETUP_DMA = s_misc.n

    pt_seq = [0]

    def pt_cycle(tr_fns, evac_fn, pe_waits=(), ev_waits=()):
        par = pt_seq[0] % 2
        pt_seq[0] += 1
        for i, f in enumerate(tr_fns):
            w = list(pe_waits) if i == 0 else []
            if i == 0:
                w.append((s_ptf, max(0, pt_seq[0] - 2)))
            st("tensor", (lambda e, f=f, par=par: f(e, par)), waits=w, inc=s_pt)
        st("vector", (lambda e, f=evac_fn, par=par: f(e, par)),
           waits=[(s_pt, s_pt.n)] + list(ev_waits), inc=s_ptf)
        return s_ptf.n

    XnT3 = XnT[:].rearrange("p (c s) -> p c s", s=SLEN)
    for t in range(NST):
        sl = t % 2
        st("sync", lambda e, t=t, sl=sl: e.dma_start(out=xst[sl][:], in_=x_h[t * 128:(t + 1) * 128, :]),
           waits=[(s_xfree[sl], max(0, t // 2))], inc=s_xld[sl], inc_n=16)
        st("vector", lambda e, sl=sl: e.reduce_sum(lnrows[:, 0:1], xst[sl][:], axis=mybir.AxisListType.X),
           waits=[(s_xld[sl], s_xld[sl].n)], inc=s_ln)
        v0 = s_ln.n
        st("scalar", lambda e, sl=sl: e.activation(out=sqscr[:], in_=xst[sl][:], func=AF.Square,
                                                   accum_out=lnrows[:, 1:2]),
           waits=[(s_xld[sl], s_xld[sl].n), (s_ln, v0), (s_misc, SETUP_DMA)], inc=s_lna)
        a0 = s_lna.n
        st("vector", lambda e: e.tensor_scalar(out=lnrows[:, 2:3], in0=lnrows[:, 0:1],
                                               scalar1=1.0 / 1024.0, scalar2=None, op0=ALU.mult),
           waits=[(s_lna, a0)], inc=s_ln)
        st("vector", lambda e: e.tensor_scalar(out=lnrows[:, 3:4], in0=lnrows[:, 1:2],
                                               scalar1=1.0 / 1024.0, scalar2=None, op0=ALU.mult),
           waits=[(s_ln, s_ln.n)], inc=s_ln)
        st("vector", lambda e: e.tensor_tensor(out=lnrows[:, 4:5], in0=lnrows[:, 2:3],
                                               in1=lnrows[:, 2:3], op=ALU.mult),
           waits=[(s_ln, s_ln.n)], inc=s_ln)
        st("vector", lambda e: e.tensor_tensor(out=lnrows[:, 5:6], in0=lnrows[:, 3:4],
                                               in1=lnrows[:, 4:5], op=ALU.subtract),
           waits=[(s_ln, s_ln.n)], inc=s_ln)
        v1 = s_ln.n
        st("scalar", lambda e: e.activation(out=lnrows[:, 6:7], in_=lnrows[:, 5:6], func=AF.Ln,
                                            bias=LNEPS),
           waits=[(s_ln, v1), (s_lna, s_lna.n)], inc=s_lna)
        st("scalar", lambda e: e.activation(out=lnrows[:, 7:8], in_=lnrows[:, 6:7], func=AF.Exp,
                                            scale=-0.5),
           waits=[(s_lna, s_lna.n)], inc=s_lna)
        a1 = s_lna.n
        st("vector", lambda e, sl=sl: e.tensor_scalar(out=xnf[sl][:], in0=xst[sl][:],
                                                      scalar1=lnrows[:, 2:3], scalar2=lnrows[:, 7:8],
                                                      op0=ALU.subtract, op1=ALU.mult),
           waits=[(s_lna, a1), (s_pt, max(0, s_pt.n - 8))], inc=s_ln)
        st("vector", lambda e: e.nop(), waits=[(s_ln, s_ln.n)], inc=s_xfree[sl])
        vn = s_ln.n
        for half in range(2):
            trs = []
            for kk in range(4):
                k = half * 4 + kk

                def tf(e, par, sl=sl, k=k, kk=kk):
                    return e.transpose(pT[par][:, kk * 128:(kk + 1) * 128],
                                       xnf[sl][:, k * 128:(k + 1) * 128], identb[:])
                trs.append(tf)

            def ev(e, par, t=t, half=half):
                dst = XnT3[:, half * 4:(half + 1) * 4, t * 128:(t + 1) * 128]
                src = pT[par][:, 0:512].rearrange("p (a b) -> p a b", b=128)
                return e.tensor_copy(out=dst, in_=src)
            w0 = [(s_ln, vn)]
            if t == 0 and half == 0:
                w0.append((s_misc, SETUP_DMA))
            pt_cycle(trs, ev, pe_waits=w0)
    st("vector", lambda e: e.nop(), waits=[(s_ptf, s_ptf.n)], inc=s_xnt)
    LN1_XNT = s_xnt.n
    sfree("xnf1", "xnf0", "xst1", "xst0", "sqscr")

    H_ec = [salloc(f"H{i}", [128, TOWN], BF16, side="right") for i in range(8)]
    Kaug = [salloc(f"Kaug{i}", [66, SLEN], BF16, side="right") for i in range(2)]
    Qaug = [salloc(f"Qaug{i}", [66, TOWN], BF16, side="right") for i in range(2)]
    Vs = [salloc(f"Vs{i}", [128, NST * 65], BF16, side="right") for i in range(2)]
    Gt = [salloc(f"Gt{i}", [64, TOWN], BF16, side="right") for i in range(2)]
    Wst = [salloc(f"Wst{i}", [128, 256], F32, side="right") for i in range(2)]
    Wbf = [salloc(f"Wbf{i}", [128, 2048], BF16, side="right") for i in range(2)]
    VTc = [salloc(f"VTc{i}", [64, 512], BF16, side="right") for i in range(2)]
    KQsq = [salloc(f"KQsq{i}", [64, 512], BF16, side="right") for i in range(2)]
    k2row = [salloc(f"k2row{i}", [1, 512], BF16, side="right") for i in range(2)]
    Lb = [salloc(f"Lb{i}", [128, 2048], F32, side="right") for i in range(2)]
    Eb = salloc("Eb0", [128, 2048], BF16, side="right")
    wb = [salloc(f"wb{i}", [128, 2048], BF16, side="right") for i in range(2)]
    A_sb = [salloc(f"A_sb{i}", [65, 512], F32, side="right") for i in range(2)]
    rsrow = salloc("rsrow", [1, 512], F32, side="right")
    rlrow = salloc("rlrow", [1, 512], F32, side="right")
    rnrow = salloc("rnrow", [1, 512], F32, side="right")
    rnhl = salloc("rnhl", [1, 1024], BF16, side="right")
    T1b = salloc("T1b", [64, 512], F32, side="right")
    Htmp = salloc("Htmp", [64, 512], BF16, side="right")

    st("vector", lambda e: e.memset(ones_col[:], 1.0), inc=s_mset)
    st("vector", lambda e: e.memset(q25_col[:], 0.25), inc=s_mset)
    st("vector", lambda e: e.memset(i1024_col[:], 1.0 / 1024.0), inc=s_mset)
    st("vector", lambda e: e.memset(ones_rowb[:], 1.0), inc=s_mset)
    st("vector", lambda e: e.memset(Vs[0][:], 1.0), inc=s_mset)
    st("vector", lambda e: e.memset(Vs[1][:], 1.0), inc=s_mset)
    st("vector", lambda e: e.memset(Kaug[0][64:65, :], 1.0), inc=s_mset)
    st("vector", lambda e: e.memset(Kaug[1][64:65, :], 1.0), inc=s_mset)
    MSET_ALL = s_mset.n
    for i in range(2):
        for n in range(NQC):
            w = [(s_mset, MSET_ALL)] if (i == 0 and n == 0) else []
            st("gpsimd", lambda e, i=i, n=n: e.dma_start(
                out=Qaug[i][65:66, n * 512:(n + 1) * 512], in_=ones_rowb[:]),
               waits=w, inc=s_aug1, inc_n=16)
    AUG1 = s_aug1.n

    pp_seq = [0]

    def pp_cycle(pe_fns, evac_fn, pe_waits=(), ev_waits=(), ev_inc=None):
        pp_seq[0] += 1
        for i, f in enumerate(pe_fns):
            w = list(pe_waits) if i == 0 else []
            if i == 0:
                w.append((s_ppf, pp_seq[0] - 1))
            st("tensor", (lambda e, f=f: f(e)), waits=w, inc=s_pp)
        st("vector", (lambda e, f=evac_fn: f(e)),
           waits=[(s_pp, s_pp.n)] + list(ev_waits), inc=s_ppf)
        tgt = s_ppf.n
        if ev_inc is not None:
            st("vector", lambda e: e.nop(), waits=[(s_ppf, tgt)], inc=ev_inc)
        return tgt

    vt_use = [0]
    sq_use = [0]

    def proj_head_w(h):
        hb = h % 2
        st("vector", lambda e: e.nop(), waits=[(s_hbf[hb], h // 2)])
        for c in range(NCH):
            sl = c % 2
            uses = h * NCH + c
            st("sync", lambda e, h=h, c=c, sl=sl: e.dma_start(
                out=Wst[sl][:], in_=win_h[c * 128:(c + 1) * 128, h * 256:(h + 1) * 256]),
               waits=[(s_wfree[sl], max(0, (uses - 2) // 2 + 1 if uses >= 2 else 0))],
               inc=s_wld[sl], inc_n=16)
            st("vector", lambda e, c=c, sl=sl, hb=hb: e.tensor_copy(
                out=Wbf[hb][:, c * 256:(c + 1) * 256], in_=Wst[sl][:]),
               waits=[(s_wld[sl], s_wld[sl].n)], inc=s_wcast)
            st("vector", lambda e, sl=sl: e.nop(), waits=[(s_wcast, s_wcast.n)], inc=s_wfree[sl])
        return s_wcast.n

    def proj_head(h, wc_done):
        hb = h % 2

        def family(cols0, nchunks_t, evac):
            for n in range(nchunks_t):
                grp()
                fns = []
                for c in range(NCH):
                    def f(e, c=c, n=n, cols0=cols0, hb=hb):
                        return e.matmul(
                            pp[0:64, :],
                            Wbf[hb][:, c * 256 + cols0:c * 256 + cols0 + 64],
                            XnT[:, c * SLEN + n * 512:c * SLEN + (n + 1) * 512],
                            start=(c == 0), stop=(c == NCH - 1))
                    fns.append(f)
                pp_cycle(fns, evac(n), pe_waits=[(s_wcast, wc_done), (s_xnt, LN1_XNT)])

        def q_ev(n):
            def f(e, n=n, hb=hb, h=h):
                return e.tensor_scalar(out=Qaug[hb][0:64, n * 512:(n + 1) * 512],
                                       in0=pp[0:64, :],
                                       scalar1=bin_sb[0:64, 2 * h:2 * h + 1], scalar2=-2.0,
                                       op0=ALU.add, op1=ALU.mult)
            return f
        family(0, NQC, q_ev)

        def k_ev(n):
            def f(e, n=n, hb=hb, h=h):
                return e.tensor_scalar(out=Kaug[hb][0:64, n * 512:(n + 1) * 512],
                                       in0=pp[0:64, :],
                                       scalar1=bin_sb[64:128, 2 * h:2 * h + 1], scalar2=None,
                                       op0=ALU.add)
            return f
        family(64, NST // 4, k_ev)

        for n in range(NST // 4):
            grp()
            sl = n % 2
            vt_use[0] += 1
            myuse = vt_use[0]
            fns = []
            for c in range(NCH):
                def f(e, c=c, n=n, hb=hb):
                    return e.matmul(pp[0:64, :],
                                    Wbf[hb][:, c * 256 + 128:c * 256 + 192],
                                    XnT[:, c * SLEN + n * 512:c * SLEN + (n + 1) * 512],
                                    start=(c == 0), stop=(c == NCH - 1))
                fns.append(f)

            def v_ev(e, n=n, sl=sl, h=h):
                return e.tensor_scalar(out=VTc[sl][:], in0=pp[0:64, :],
                                       scalar1=bin_sb[0:64, 2 * h + 1:2 * h + 2], scalar2=None,
                                       op0=ALU.add)
            vtgt = pp_cycle(fns, v_ev,
                            pe_waits=[(s_wcast, wc_done), (s_xnt, LN1_XNT)],
                            ev_waits=[(s_vtf[sl], max(0, (myuse - 2 - 1) // 2 + 1 if myuse > 2 else 0))])
            trs = []
            for k in range(4):
                def tf(e, par, k=k, sl=sl):
                    return e.transpose(pT[par][:, k * 64:(k + 1) * 64],
                                       VTc[sl][:, k * 128:(k + 1) * 128], identb[0:64, 0:64])
                trs.append(tf)

            def vs_ev(e, par, n=n, hb=hb):
                dst = Vs[hb][:].rearrange("p (st c) -> p st c", c=65)[:, n * 4:(n + 1) * 4, 0:64]
                src = pT[par][:, 0:256].rearrange("p (a b) -> p a b", b=64)
                return e.tensor_copy(out=dst, in_=src)
            pt_cycle(trs, vs_ev, pe_waits=[(s_ppf, vtgt)])
            st("vector", lambda e: e.nop(), waits=[(s_ptf, s_ptf.n)], inc=s_vtf[sl])

        def g_ev(n):
            def f(e, n=n, hb=hb, h=h):
                return e.tensor_scalar(out=Gt[hb][:, n * 512:(n + 1) * 512],
                                       in0=pp[0:64, :],
                                       scalar1=bin_sb[64:128, 2 * h + 1:2 * h + 2], scalar2=None,
                                       op0=ALU.add)
            return f
        family(192, NQC, g_ev)

        for kind, srcb, nn, scol in (("k2", Kaug[hb], NST // 4, ones_col),
                                     ("q2", Qaug[hb], NQC, q25_col)):
            for n in range(nn):
                grp()
                sq_use[0] += 1
                myuse = sq_use[0]
                sl = myuse % 2
                st("vector", lambda e, srcb=srcb, n=n, sl=sl: e.tensor_tensor(
                    out=KQsq[sl][:], in0=srcb[0:64, n * 512:(n + 1) * 512],
                    in1=srcb[0:64, n * 512:(n + 1) * 512], op=ALU.mult),
                   waits=[(s_sqf[sl], max(0, (myuse - 2 - 1) // 2 + 1 if myuse > 2 else 0))],
                   inc=s_sqt)
                sqtn = s_sqt.n

                def mm(e, sl=sl, scol=scol):
                    return e.matmul(pp[0:1, :], scol[0:64, :], KQsq[sl][:],
                                    start=True, stop=True)

                def ev(e, sl=sl):
                    return e.tensor_copy(out=k2row[sl][:], in_=pp[0:1, :])
                pp_cycle([mm], ev, pe_waits=[(s_sqt, sqtn)],
                         ev_waits=[(s_k2dma[sl], s_k2dma[sl].n)], ev_inc=s_sqf[sl])
                k2t = s_sqf[sl].n
                if kind == "k2":
                    st("gpsimd", lambda e, hb=hb, n=n, sl=sl: e.dma_start(
                        out=Kaug[hb][65:66, n * 512:(n + 1) * 512], in_=k2row[sl][:]),
                       waits=[(s_sqf[sl], k2t), (s_hbf[hb], h // 2), (s_aug1, AUG1)],
                       inc=s_k2dma[sl], inc_n=16)
                else:
                    st("gpsimd", lambda e, hb=hb, n=n, sl=sl: e.dma_start(
                        out=Qaug[hb][64:65, n * 512:(n + 1) * 512], in_=k2row[sl][:]),
                       waits=[(s_sqf[sl], k2t), (s_hbf[hb], h // 2)],
                       inc=s_k2dma[sl], inc_n=16)

        st("vector", lambda e: e.nop(), waits=[(s_ppf, s_ppf.n), (s_ptf, s_ptf.n)], inc=s_prj)
        st("gpsimd", lambda e: e.nop(),
           waits=[(s_k2dma[0], s_k2dma[0].n), (s_k2dma[1], s_k2dma[1].n), (s_prj, s_prj.n)],
           inc=s_hb[hb])
        return s_hb[hb].n

    NTOT = HEADS * NQC * NREG

    def rinfo(r):
        return r // (NQC * NREG), (r // NREG) % NQC, r % NREG

    h_ready = [0, 0]

    staging_groups = None
    stagingW = None
    moved_groups = [0]

    def grp():
        if staging_groups is not None:
            staging_groups.append({e: [] for e in ENGINES})
            cur_steps[0] = staging_groups[-1]

    def stage_proj(h):
        nonlocal staging_groups, stagingW
        stagingW = {e: [] for e in ENGINES}
        cur_steps[0] = stagingW
        wc = proj_head_w(h)
        staging_groups = [{e: [] for e in ENGINES}]
        cur_steps[0] = staging_groups[-1]
        hr = proj_head(h, wc)
        cur_steps[0] = steps
        moved_groups[0] = 0
        return hr

    def flush_w():
        nonlocal stagingW
        if stagingW is None:
            return
        for e in ENGINES:
            steps[e].extend(stagingW[e])
        stagingW = None

    def splice(frac_num, frac_den):
        if staging_groups is None:
            return
        target = len(staging_groups) * frac_num // frac_den
        for g in staging_groups[moved_groups[0]:target]:
            for e in ("tensor", "vector", "scalar", "gpsimd", "sync"):
                steps[e].extend(g[e])
        moved_groups[0] = max(moved_groups[0], target)

    h_ready[0] = proj_head(0, proj_head_w(0))
    prev_e4 = 0

    def emit_S(r):
        h, qc, g = rinfo(r)
        hb = h % 2
        ps = r % 2
        for k in range(2):
            stile = 2 * g + k
            w = [(s_hb[hb], h_ready[hb])]
            if k == 0 and r >= 2:
                w.append((s_c1, r - 1))
            st("tensor", lambda e, hb=hb, qc=qc, stile=stile, ps=ps, k=k: e.matmul(
                pS[ps][:, k * 512:(k + 1) * 512],
                Kaug[hb][:, stile * 128:(stile + 1) * 128],
                Qaug[hb][:, qc * 512:(qc + 1) * 512],
                start=True, stop=True),
               waits=w, inc=s_peS)

    def emit_c1(r):
        s = r // 2
        ps = r % 2
        w = [(s_peS, 2 * r + 2)]
        if s >= 2:
            w.append((s_c4, s - 1))
        st("scalar", lambda e, ps=ps, s=s, r=r: e.activation(
            out=Lb[s % 2][:, (r % 2) * 1024:(r % 2) * 1024 + 1024],
            in_=pS[ps][:], func=AF.Ln), waits=w, inc=s_c1)

    def emit_c2(s):
        w = [(s_c1, 2 * s + 2)]
        if s >= 1:
            w.append((s_c3, s))
        st("scalar", lambda e, s=s: e.activation(
            out=Eb[:], in_=Lb[s % 2][:], func=AF.Exp, scale=-0.5, bias=LN2EPS),
           waits=w, inc=s_c2)

    def emit_c3(s):
        st("vector", lambda e, s=s: e.tensor_tensor(
            out=Lb[s % 2][:], in0=Lb[s % 2][:], in1=Eb[:], op=ALU.add),
           waits=[(s_c2, s + 1)], inc=s_c3)

    def emit_c4(s):
        w = [(s_c3, s + 1)]
        if s >= 2:
            w.append((s_wcons, 4 * s - 4))
        st("scalar", lambda e, s=s: e.activation(
            out=wb[s % 2][:], in_=Lb[s % 2][:], func=AF.Exp, scale=-1.0),
           waits=w, inc=s_c4)

    def emit_A(r):
        h, qc, g = rinfo(r)
        hb = h % 2
        s = r // 2
        for k in range(2):
            stile = 2 * g + k
            w = [(s_c4, s + 1)]
            if stile == 0:
                w.append((s_epi, prev_e4))
            st("tensor", lambda e, hb=hb, stile=stile, ws=s % 2, half=r % 2, k=k: e.matmul(
                pA[0:65, :],
                Vs[hb][:, stile * 65:stile * 65 + 65],
                wb[ws][:, half * 1024 + k * 512:half * 1024 + (k + 1) * 512],
                start=(stile == 0), stop=(stile == NST - 1)),
               waits=w, inc=s_wcons)

    def emit_epilogue(r_end):
        nonlocal prev_e4
        h, qc, g = rinfo(r_end)
        hb = h % 2
        n_epi = h * NQC + qc
        asl = n_epi % 2
        st("vector", lambda e, asl=asl: e.tensor_copy(out=A_sb[asl][:], in_=pA[0:65, :]),
           waits=[(s_wcons, 2 * (r_end + 1)), (s_epi, max(0, s_epi.n - 5)),
                  (s_rsd, max(0, s_rsd.n - 16))], inc=s_e1)
        e1n = s_e1.n
        st("gpsimd", lambda e, asl=asl: e.dma_start(out=rsrow[:], in_=A_sb[asl][64:65, :]),
           waits=[(s_e1, e1n), (s_rn, s_rn.n)], inc=s_rsd, inc_n=16)
        rsn = s_rsd.n
        st("scalar", lambda e: e.activation(out=rlrow[:], in_=rsrow[:], func=AF.Ln, bias=EPS),
           waits=[(s_rsd, rsn)], inc=s_rn)
        st("scalar", lambda e: e.activation(out=rnrow[:], in_=rlrow[:], func=AF.Exp, scale=-1.0),
           waits=[(s_rn, s_rn.n)], inc=s_rn)
        rnn = s_rn.n
        st("vector", lambda e: e.tensor_copy(out=rnhl[:, 0:512], in_=rnrow[:]),
           waits=[(s_rn, rnn), (s_bc, s_bc.n)], inc=s_epi)
        st("vector", lambda e: e.tensor_tensor(out=rnhl[:, 512:1024], in0=rnrow[:],
                                               in1=rnhl[:, 0:512], op=ALU.subtract),
           waits=[(s_epi, s_epi.n)], inc=s_epi)
        spl = s_epi.n
        w = [(s_epi, spl), (s_e1, e1n)]
        if n_epi == 0:
            w.append((s_xnt, LN1_XNT))
        st("tensor", lambda e: e.matmul(pA[0:64, :], ones_rowb[:, 0:64], rnhl[:, 0:512],
                                        start=True, stop=False),
           waits=w, inc=s_bc)
        st("tensor", lambda e: e.matmul(pA[0:64, :], ones_rowb[:, 0:64], rnhl[:, 512:1024],
                                        start=False, stop=True),
           waits=[], inc=s_bc)
        bcn = s_bc.n
        st("vector", lambda e, asl=asl: e.tensor_tensor(out=T1b[:], in0=A_sb[asl][0:64, :],
                                                        in1=pA[0:64, :], op=ALU.mult),
           waits=[(s_bc, bcn)], inc=s_epi)
        prev_e4 = s_epi.n
        ec = h // 2
        if h % 2 == 0:
            st("vector", lambda e, hb=hb, qc=qc, ec=ec: e.tensor_tensor(
                out=H_ec[ec][0:64, qc * 512:(qc + 1) * 512],
                in0=T1b[:], in1=Gt[hb][:, qc * 512:(qc + 1) * 512], op=ALU.mult),
               waits=[(s_epi, s_epi.n)], inc=s_epi)
        else:
            st("vector", lambda e, hb=hb, qc=qc: e.tensor_tensor(
                out=Htmp[:], in0=T1b[:], in1=Gt[hb][:, qc * 512:(qc + 1) * 512], op=ALU.mult),
               waits=[(s_epi, s_epi.n), (s_hdma, s_hdma.n)], inc=s_epi)
            epn = s_epi.n
            st("gpsimd", lambda e, qc=qc, ec=ec: e.dma_start(
                out=H_ec[ec][64:128, qc * 512:(qc + 1) * 512], in_=Htmp[:]),
               waits=[(s_epi, epn)], inc=s_hdma, inc_n=16)
        if qc == NQC - 1:
            st("vector", lambda e: e.nop(), waits=[(s_epi, s_epi.n)], inc=s_hbf[hb])

    emit_S(0)
    for r in range(NTOT):
        h, qc, g = rinfo(r)
        if g == 0 and qc == 0:
            if h + 1 < HEADS:
                h_ready[(h + 1) % 2] = stage_proj(h + 1)
            else:
                staging_groups = None
        emit_c1(r)
        if r % 2 == 1:
            emit_c2(r // 2)
            emit_c3(r // 2)
        nxt_head_start = (r + 1) % (NQC * NREG) == 0
        if r + 1 < NTOT and not nxt_head_start:
            emit_S(r + 1)
        if r % 2 == 0 and r >= 2:
            s = r // 2
            emit_c4(s - 1)
            emit_A(2 * s - 2)
            emit_A(2 * s - 1)
            hh, qq, gg = rinfo(2 * s - 1)
            if gg == NREG - 1:
                emit_epilogue(2 * s - 1)
        slot_in_head = r % (NQC * NREG)
        if slot_in_head == 0:
            flush_w()
        splice(slot_in_head + 1, NQC * NREG)
        if r + 1 < NTOT and nxt_head_start:
            emit_S(r + 1)
    emit_c4(NTOT // 2 - 1)
    emit_A(NTOT - 2)
    emit_A(NTOT - 1)
    emit_epilogue(NTOT - 1)

    ATT_EPI = s_epi.n
    ATT_HDMA = s_hdma.n
    ATT_E1 = s_e1.n
    ATT_E4 = prev_e4
    CH_C1 = s_c1.n

    sfree("Htmp", "T1b", "rnhl", "rnrow", "rlrow", "rsrow", "A_sb1", "A_sb0",
          "wb1", "wb0", "Eb0", "Lb1", "Lb0",
          "k2row1", "k2row0", "KQsq1", "KQsq0", "VTc1", "VTc0",
          "Wbf1", "Wbf0", "Wst1", "Wst0", "Gt1", "Gt0", "Vs1", "Vs0",
          "Qaug1", "Qaug0", "Kaug1", "Kaug0")
    sfree("XnT")

    wst2 = [salloc(f"wst2_{i}", [128, D], F32, side="right") for i in range(2)]
    Wout_bf = salloc("Wout_bf", [128, 8 * D], BF16, side="right")
    XT = salloc("XT", [128, NCH * TOWN], F32)
    x1T = salloc("x1T", [128, NCH * TOWN], F32)

    for c in range(NCH):
        sl = c % 2
        st("sync", lambda e, c=c, sl=sl: e.dma_start(out=wst2[sl][:], in_=wout_h[c * 128:(c + 1) * 128, :]),
           waits=[(s_peS, s_peS.n), (s_wo, max(0, s_wo.n - 1))], inc=s_xld[sl], inc_n=16)
        st("vector", lambda e, c=c, sl=sl: e.tensor_copy(out=Wout_bf[:, c * 1024:(c + 1) * 1024],
                                                         in_=wst2[sl][:]),
           waits=[(s_xld[sl], s_xld[sl].n), (s_epi, ATT_EPI)], inc=s_wo)
    WO_DONE = s_wo.n
    for c in range(NCH):
        st("sync", lambda e, c=c: e.dma_start(out=XT[:, c * TOWN:(c + 1) * TOWN],
                                              in_=xT_h[c * 128:(c + 1) * 128, :]),
           waits=[], inc=s_xt, inc_n=16)
    XT_DONE = s_xt.n

    for dt in range(NCH):
        for tc in range(NQC):
            slot = (dt * NQC + tc) % 4
            pso = pS[slot // 2]
            off = (slot % 2) * 512
            for ec in range(NCH):
                w = []
                if ec == 0:
                    w = [(s_wo, WO_DONE), (s_hdma, ATT_HDMA), (s_epi, ATT_EPI),
                         (s_c1, CH_C1), (s_x1, max(0, s_x1.n - 3))]
                st("tensor", lambda e, dt=dt, tc=tc, ec=ec, pso=pso, off=off: e.matmul(
                    pso[:, off:off + 512],
                    Wout_bf[:, ec * 1024 + dt * 128:ec * 1024 + dt * 128 + 128],
                    H_ec[ec][:, tc * 512:(tc + 1) * 512],
                    start=(ec == 0), stop=(ec == NCH - 1)),
                   waits=w, inc=s_op)
            opn = s_op.n
            st("vector", lambda e, dt=dt, tc=tc, pso=pso, off=off: e.scalar_tensor_tensor(
                out=x1T[:, dt * TOWN + tc * 512:dt * TOWN + (tc + 1) * 512],
                in0=pso[:, off:off + 512], scalar=bout_sb[:, dt:dt + 1],
                in1=XT[:, dt * TOWN + tc * 512:dt * TOWN + (tc + 1) * 512],
                op0=ALU.add, op1=ALU.add),
               waits=[(s_op, opn), (s_xt, XT_DONE)], inc=s_x1)
    X1_DONE = s_x1.n

    sfree("Wout_bf", "wst2_1", "wst2_0")
    sfree("H7", "H6", "H5", "H4", "H3", "H2", "H1", "H0")
    signWT = salloc("signWT", [128, 8 * D], BF16, side="right")
    wst3 = [salloc(f"wst3_{i}", [128, D], F32, side="right") for i in range(2)]
    sq2 = salloc("sq2", [128, 512], F32, side="right")
    x1n_bf = salloc("x1n_bf", [128, NCH * TOWN], BF16, side="right")
    sgb = salloc("sgb", [128, D], BF16, side="right")
    sqb = salloc("sqb", [128, 512], BF16, side="right")
    srows = salloc("srows", [1, 2560], F32, side="right")
    srow = {"m": srows[:, 0:512], "q": srows[:, 512:1024], "mm": srows[:, 1024:1536],
            "var": srows[:, 1536:2048], "lv": srows[:, 1024:1536],
            "rstd": srows[:, 2048:2560], "b": srows[:, 512:1024]}
    abhl = salloc("abhl", [1, 2048], BF16, side="right")

    sWT3 = signWT[:].rearrange("p (c o) -> p c o", o=D)
    for oc in range(NCH):
        sl = oc % 2
        st("sync", lambda e, oc=oc, sl=sl: e.dma_start(out=wst3[sl][:], in_=wffn_h[oc * 128:(oc + 1) * 128, :]),
           waits=[(s_sg, max(0, s_sg.n - 1)), (s_x1, X1_DONE)], inc=s_xld[sl], inc_n=16)
        st("scalar", lambda e, sl=sl: e.activation(out=sgb[:], in_=wst3[sl][:], func=AF.Sign),
           waits=[(s_xld[sl], s_xld[sl].n), (s_ptf, s_ptf.n),
                  (s_x1, X1_DONE)], inc=s_sg)
        sgn = s_sg.n
        for half in range(2):
            trs = []
            for kk in range(4):
                k = half * 4 + kk

                def tf(e, par, k=k, kk=kk):
                    return e.transpose(pT[par][:, kk * 128:(kk + 1) * 128],
                                       sgb[:, k * 128:(k + 1) * 128], identb[:])
                trs.append(tf)

            def ev(e, par, oc=oc, half=half):
                dst = sWT3[:, half * 4:(half + 1) * 4, oc * 128:(oc + 1) * 128]
                src = pT[par][:, 0:512].rearrange("p (a b) -> p a b", b=128)
                return e.tensor_copy(out=dst, in_=src)
            pt_cycle(trs, ev, pe_waits=[(s_sg, sgn)])
    st("vector", lambda e: e.nop(), waits=[(s_ptf, s_ptf.n)], inc=s_sg)
    SG_DONE = s_sg.n

    for tc in range(NQC):
        for c in range(NCH):
            a0 = c * TOWN + tc * 512
            a1_ = c * TOWN + (tc + 1) * 512
            st("vector", lambda e, a0=a0, a1_=a1_: e.tensor_copy(
                out=x1n_bf[:, a0:a1_], in_=x1T[:, a0:a1_]),
               waits=[(s_x1, X1_DONE)], inc=s_xbf)
            xbn = s_xbf.n
            w0 = [(s_xbf, xbn)]
            if c == 0:
                w0 += [(s_str, s_str.n), (s_abf, s_abf.n)]
                if tc == 0:
                    w0 += [(s_e1, ATT_E1), (s_epi, ATT_EPI), (s_fp, 0),
                           (s_x1, X1_DONE)]
            st("tensor", lambda e, a0=a0, a1_=a1_, c=c: e.matmul(
                pA[0:1, :], i1024_col[:], x1n_bf[:, a0:a1_],
                start=(c == 0), stop=(c == NCH - 1)),
               waits=[wi for wi in w0 if wi[1] > 0], inc=s_st1)
            st("vector", lambda e, a0=a0, a1_=a1_: e.tensor_tensor(
                out=sqb[:], in0=x1n_bf[:, a0:a1_], in1=x1n_bf[:, a0:a1_], op=ALU.mult),
               waits=[(s_xbf, xbn), (s_st1, s_st1.n)], inc=s_xbf)
            st("tensor", lambda e, c=c: e.matmul(
                pS[1][0:1, 0:512], i1024_col[:], sqb[:],
                start=(c == 0), stop=(c == NCH - 1)),
               waits=[(s_xbf, s_xbf.n)], inc=s_st1)
        stn = s_st1.n
        st("vector", lambda e: e.tensor_copy(out=srow["m"], in_=pA[0:1, :]),
           waits=[(s_st1, stn)], inc=s_str)
        st("vector", lambda e: e.tensor_copy(out=srow["q"], in_=pS[1][0:1, 0:512]),
           waits=[(s_str, s_str.n)], inc=s_str)
        st("vector", lambda e: e.tensor_tensor(out=srow["mm"], in0=srow["m"], in1=srow["m"],
                                               op=ALU.mult),
           waits=[(s_str, s_str.n)], inc=s_str)
        st("vector", lambda e: e.tensor_tensor(out=srow["var"], in0=srow["q"], in1=srow["mm"],
                                               op=ALU.subtract),
           waits=[(s_str, s_str.n)], inc=s_str)
        vrn = s_str.n
        st("scalar", lambda e: e.activation(out=srow["lv"], in_=srow["var"], func=AF.Ln,
                                            bias=LNEPS),
           waits=[(s_str, vrn), (s_sta, s_sta.n)], inc=s_sta)
        st("scalar", lambda e: e.activation(out=srow["rstd"], in_=srow["lv"], func=AF.Exp,
                                            scale=-0.5),
           waits=[(s_sta, s_sta.n)], inc=s_sta)
        san = s_sta.n
        st("vector", lambda e: e.scalar_tensor_tensor(out=srow["b"], in0=srow["m"],
                                                      scalar=-1.0, in1=srow["rstd"],
                                                      op0=ALU.mult, op1=ALU.mult),
           waits=[(s_sta, san), (s_str, s_str.n)], inc=s_str)
        st("vector", lambda e: e.tensor_copy(out=abhl[:, 0:512], in_=srow["rstd"]),
           waits=[(s_str, s_str.n), (s_abf, s_abf.n)], inc=s_str)
        st("vector", lambda e: e.tensor_tensor(out=abhl[:, 512:1024], in0=srow["rstd"],
                                               in1=abhl[:, 0:512], op=ALU.subtract),
           waits=[(s_str, s_str.n)], inc=s_str)
        st("vector", lambda e: e.tensor_copy(out=abhl[:, 1024:1536], in_=srow["b"]),
           waits=[(s_str, s_str.n)], inc=s_str)
        st("vector", lambda e: e.tensor_tensor(out=abhl[:, 1536:2048], in0=srow["b"],
                                               in1=abhl[:, 1024:1536], op=ALU.subtract),
           waits=[(s_str, s_str.n)], inc=s_str)
        brn = s_str.n
        st("tensor", lambda e: e.matmul(pA[:, 0:512], ones_rowb[:, 0:128], abhl[:, 0:512],
                                        start=True, stop=False),
           waits=[(s_str, brn)], inc=s_ab)
        st("tensor", lambda e: e.matmul(pA[:, 0:512], ones_rowb[:, 0:128], abhl[:, 512:1024],
                                        start=False, stop=True),
           waits=[], inc=s_ab)
        st("tensor", lambda e: e.matmul(pS[0][:, 0:512], ones_rowb[:, 0:128], abhl[:, 1024:1536],
                                        start=True, stop=False),
           waits=[], inc=s_ab)
        st("tensor", lambda e: e.matmul(pS[0][:, 0:512], ones_rowb[:, 0:128], abhl[:, 1536:2048],
                                        start=False, stop=True),
           waits=[], inc=s_ab)
        abn = s_ab.n
        for c in range(NCH):
            a0 = c * TOWN + tc * 512
            a1_ = c * TOWN + (tc + 1) * 512
            st("vector", lambda e, a0=a0, a1_=a1_: e.tensor_tensor(
                out=sq2[:, 0:512], in0=x1T[:, a0:a1_], in1=pA[:, 0:512], op=ALU.mult),
               waits=[(s_ab, abn), (s_x1n, s_x1n.n)], inc=s_x1n)
            st("vector", lambda e, a0=a0, a1_=a1_: e.tensor_tensor(
                out=x1n_bf[:, a0:a1_], in0=sq2[:, 0:512], in1=pS[0][:, 0:512], op=ALU.add),
               waits=[(s_x1n, s_x1n.n)], inc=s_x1n)
        st("vector", lambda e: e.nop(), waits=[(s_x1n, s_x1n.n)], inc=s_abf)
    X1N_DONE = s_x1n.n

    x2T = XT
    for ot in range(NCH):
        for tc in range(NQC):
            slot = (ot * NQC + tc) % 4
            pso = pS[slot // 2]
            off = (slot % 2) * 512
            for ic in range(NCH):
                w = []
                if ic == 0:
                    w = [(s_sg, SG_DONE), (s_x1n, X1N_DONE), (s_op, s_op.n),
                         (s_x2, max(0, s_x2.n - 3)), (s_abf, s_abf.n)]
                st("tensor", lambda e, ot=ot, tc=tc, ic=ic, pso=pso, off=off: e.matmul(
                    pso[:, off:off + 512],
                    signWT[:, ic * 1024 + ot * 128:ic * 1024 + ot * 128 + 128],
                    x1n_bf[:, ic * TOWN + tc * 512:ic * TOWN + (tc + 1) * 512],
                    start=(ic == 0), stop=(ic == NCH - 1)),
                   waits=w, inc=s_fp)
            fpn = s_fp.n
            st("vector", lambda e, ot=ot, tc=tc, pso=pso, off=off: e.tensor_tensor(
                out=x2T[:, ot * TOWN + tc * 512:ot * TOWN + (tc + 1) * 512],
                in0=pso[:, off:off + 512],
                in1=x1T[:, ot * TOWN + tc * 512:ot * TOWN + (tc + 1) * 512], op=ALU.add),
               waits=[(s_fp, fpn), (s_x1, X1_DONE), (s_xt, XT_DONE)], inc=s_x2)
            x2n = s_x2.n
            st("sync", lambda e, ot=ot, tc=tc: e.dma_start(
                out=y_h[ot * 128:(ot + 1) * 128, tc * 512:(tc + 1) * 512],
                in_=x2T[:, ot * TOWN + tc * 512:ot * TOWN + (tc + 1) * 512]),
               waits=[(s_x2, x2n)], inc=s_store, inc_n=16)
    st("sync", lambda e: e.nop(), waits=[(s_store, s_store.n)])

    with contextlib.ExitStack() as es2:
        for s in sems:
            s.h = es2.enter_context(nc.semaphore(s.name))
        block = es2.enter_context(nc.Block())

        def emitter(name):
            def run(eng):
                for waits, fn, inc, inc_n in steps[name]:
                    for (sm, val) in waits:
                        if val > 0:
                            eng.wait_ge(sm.h, val)
                    inst = fn(eng)
                    if inc is not None:
                        inst.then_inc(inc.h, inc_n)
            return run

        block.sync(emitter("sync"))
        block.tensor(emitter("tensor"))
        block.vector(emitter("vector"))
        block.scalar(emitter("scalar"))
        block.gpsimd(emitter("gpsimd"))

    for g in reversed(list(guards.values())):
        g.__exit__(None, None, None)
    guards.clear()
    return nc


_NC_CACHE = None


def kernel(L0_attn_X, L0_attn_W_in, L0_attn_b_in, L0_attn_W_out, L0_attn_b_out, L0_ffn_weight):
    global _NC_CACHE
    X = np.asarray(L0_attn_X, dtype=np.float32)
    W_in = np.ascontiguousarray(np.asarray(L0_attn_W_in, dtype=np.float32))
    b_in = np.asarray(L0_attn_b_in, dtype=np.float32)
    W_out = np.ascontiguousarray(np.asarray(L0_attn_W_out, dtype=np.float32))
    b_out = np.asarray(L0_attn_b_out, dtype=np.float32)
    W_ffn = np.ascontiguousarray(np.asarray(L0_ffn_weight, dtype=np.float32))

    import ml_dtypes
    bin_t = np.ascontiguousarray(b_in.reshape(32, 128).T)
    bout_t = np.ascontiguousarray(b_out.reshape(8, 128).T)
    identb = np.eye(128, dtype=np.float32).astype(ml_dtypes.bfloat16)
    cstv = np.broadcast_to(np.array([[LN2EPS, LNEPS, EPS, -0.5, -1.0, 0.25, 1.0, 0.0]],
                                    dtype=np.float32), (128, 8)).copy()

    if _NC_CACHE is None:
        _NC_CACHE = build_nc()
    nc = _NC_CACHE

    in_maps = []
    for c in range(8):
        b, half = c // 2, c % 2
        xb = X[b] if half == 0 else np.roll(X[b], -TOWN, axis=0)
        xb = np.ascontiguousarray(xb)
        in_maps.append({
            "x": xb, "xT": np.ascontiguousarray(xb[0:TOWN].T),
            "w_in": W_in, "b_in_t": bin_t, "w_out": W_out, "b_out_t": bout_t,
            "w_ffn": W_ffn, "identb": identb, "cst": cstv,
        })
    res = run_bass_kernel_spmd(nc, in_maps, core_ids=list(range(8)))
    out = np.empty((B, T, D), dtype=np.float32)
    for c in range(8):
        b, half = c // 2, c % 2
        out[b, half * TOWN:(half + 1) * TOWN, :] = res.results[c]["y"].T
    return out


# version 15
# speedup vs baseline: 1.2077x; 1.2077x over previous
import contextlib
import numpy as np

import concourse.bass as bass
import concourse.mybir as mybir
from concourse.bass_utils import run_bass_kernel_spmd
from concourse.alu_op_type import AluOpType as ALU

AF = mybir.ActivationFunctionType
F32 = mybir.dt.float32
BF16 = mybir.dt.bfloat16

B, T, D = 4, 4096, 1024
HEADS = 16
TOWN = 2048
SLEN = 4096
NQC = 4
NST = 32
NREG = 16
EPS = 1e-4
LN2EPS = float(np.log(2.0 * EPS))
LNEPS = 1e-5
NCH = 8

ENGINES = ("sync", "tensor", "vector", "scalar", "gpsimd")


class Sem:
    __slots__ = ("name", "n", "h")

    def __init__(self, name):
        self.name, self.n, self.h = name, 0, None


def build_nc():
    nc = bass.Bass("TRN2", target_bir_lowering=False)
    steps = {e: [] for e in ENGINES}
    sems = []

    def mksem(name):
        s = Sem(name)
        sems.append(s)
        return s

    cur_steps = [steps]

    def st(engine, fn, waits=(), inc=None, inc_n=1):
        if inc is not None:
            inc.n += inc_n
        cur_steps[0][engine].append((list(waits), fn, inc, inc_n))

    x_h = nc.dram_tensor("x", [SLEN, D], F32, kind="ExternalInput").ap()
    xT_h = nc.dram_tensor("xT", [D, TOWN], F32, kind="ExternalInput").ap()
    win_h = nc.dram_tensor("w_in", [D, HEADS * 256], F32, kind="ExternalInput").ap()
    bin_h = nc.dram_tensor("b_in_t", [128, 32], F32, kind="ExternalInput").ap()
    wout_h = nc.dram_tensor("w_out", [D, D], F32, kind="ExternalInput").ap()
    bout_h = nc.dram_tensor("b_out_t", [128, 8], F32, kind="ExternalInput").ap()
    wffn_h = nc.dram_tensor("w_ffn", [D, D], F32, kind="ExternalInput").ap()
    identb_h = nc.dram_tensor("identb", [128, 128], BF16, kind="ExternalInput").ap()
    cst_h = nc.dram_tensor("cst", [128, 8], F32, kind="ExternalInput").ap()
    y_h = nc.dram_tensor("y", [D, TOWN], F32, kind="ExternalOutput").ap()

    guards = {}

    def salloc(name, shape, dtype, side="left"):
        g = nc.sbuf_tensor(name, shape, dtype, side=side)
        t = g.__enter__()
        guards[name] = g
        return t

    def sfree(*names):
        for nm in names:
            guards.pop(nm).__exit__(None, None, None)

    def palloc(name, shape, dtype, side="left"):
        g = nc.psum_tensor(name, shape, dtype, side=side)
        t = g.__enter__()
        guards[name] = g
        return t

    cst = salloc("c_cst", [128, 8], F32)
    identb = salloc("c_identb", [128, 128], BF16)
    bin_sb = salloc("bin_sb", [128, 32], F32)
    bout_sb = salloc("bout_sb", [128, 8], F32)
    ones_col = salloc("ones_col", [128, 1], BF16)
    q25_col = salloc("q25_col", [128, 1], BF16)
    i1024_col = salloc("i1024_col", [128, 1], BF16)
    ones_rowb = salloc("ones_rowb", [1, 512], BF16)
    lnrows = salloc("lnrows", [128, 8], F32)
    sqscr = salloc("sqscr", [128, D], F32)
    xst = [salloc(f"xst{i}", [128, D], F32) for i in range(2)]

    XnT = salloc("XnT", [128, NCH * SLEN], BF16)
    xnf = [salloc(f"xnf{i}", [128, D], BF16, side="right") for i in range(2)]

    pp = palloc("pp", [128, 512], F32)
    pS = [palloc(f"pS{i}", [128, 1024], F32, side="right") for i in range(2)]
    pT = [palloc(f"pT{i}", [128, 512], BF16, side="right") for i in range(2)]

    nc.const_aps.register(F32, LN2EPS, cst[:, 0:1])
    nc.const_aps.register(F32, LNEPS, cst[:, 1:2])
    nc.const_aps.register(F32, EPS, cst[:, 2:3])
    nc.const_aps.register(F32, -0.5, cst[:, 3:4])
    nc.const_aps.register(F32, -1.0, cst[:, 4:5])
    nc.const_aps.register(F32, 0.25, cst[:, 5:6])
    nc.const_aps.register(F32, 1.0, cst[:, 6:7])
    nc.const_aps.register(F32, 0.0, cst[:, 7:8])

    s_misc = mksem("misc")
    s_mset = mksem("mset")
    s_aug1 = mksem("aug1")
    s_xld = [mksem("xld0"), mksem("xld1")]
    s_xfree = [mksem(f"xfree{i}") for i in range(2)]
    s_ln = mksem("ln")
    s_lna = mksem("lna")
    s_pt = mksem("pt")
    s_ptf = mksem("ptf")
    s_xnt = mksem("xnt")
    s_wld = [mksem("wld0"), mksem("wld1")]
    s_wcast = mksem("wcast")
    s_wfree = [mksem(f"wfree{i}") for i in range(2)]
    s_pp = mksem("pp")
    s_ppf = mksem("ppf")
    s_prj = mksem("prj")
    s_sqt = mksem("sqt")
    s_vtf = [mksem(f"vtf{i}") for i in range(2)]
    s_sqf = [mksem(f"sqf{i}") for i in range(2)]
    s_k2dma = [mksem("k2dma0"), mksem("k2dma1")]
    s_hb = [mksem(f"hb{i}") for i in range(2)]
    s_hbf = [mksem(f"hbf{i}") for i in range(2)]
    s_peS = mksem("peS")
    s_c1 = mksem("c1")
    s_c2 = mksem("c2")
    s_c3 = mksem("c3")
    s_c4 = mksem("c4")
    s_wcons = mksem("wcons")
    s_e1 = mksem("e1")
    s_rsd = mksem("rsd")
    s_rn = mksem("rn")
    s_bc = mksem("bc")
    s_epi = mksem("epi")
    s_hdma = mksem("hdma")
    s_wo = mksem("wo")
    s_sg = mksem("sg")
    s_xt = mksem("xt")
    s_op = mksem("op")
    s_x1 = mksem("x1")
    s_st1 = mksem("st1")
    s_str = mksem("str")
    s_sta = mksem("sta")
    s_xbf = mksem("xbf")
    s_ab = mksem("ab")
    s_abf = mksem("abf")
    s_x1n = mksem("x1n")
    s_fp = mksem("fp")
    s_x2 = mksem("x2")
    s_store = mksem("store")

    st("sync", lambda e: e.dma_start(out=cst[:], in_=cst_h), inc=s_misc, inc_n=16)
    st("sync", lambda e: e.dma_start(out=identb[:], in_=identb_h), inc=s_misc, inc_n=16)
    st("sync", lambda e: e.dma_start(out=bin_sb[:], in_=bin_h), inc=s_misc, inc_n=16)
    st("sync", lambda e: e.dma_start(out=bout_sb[:], in_=bout_h), inc=s_misc, inc_n=16)
    SETUP_DMA = s_misc.n

    pt_seq = [0]

    def pt_cycle(tr_fns, evac_fn, pe_waits=(), ev_waits=()):
        par = pt_seq[0] % 2
        ptt = pT[par]
        pt_seq[0] += 1
        for i, f in enumerate(tr_fns):
            w = list(pe_waits) if i == 0 else []
            if i == 0:
                w.append((s_ptf, max(0, pt_seq[0] - 2)))
            st("tensor", (lambda e, f=f, ptt=ptt: f(e, ptt)), waits=w, inc=s_pt)
        st("vector", (lambda e, f=evac_fn, ptt=ptt: f(e, ptt)),
           waits=[(s_pt, s_pt.n)] + list(ev_waits), inc=s_ptf)
        return s_ptf.n

    XnT3 = XnT[:].rearrange("p (c s) -> p c s", s=SLEN)
    for t in range(NST):
        sl = t % 2
        st("sync", lambda e, t=t, sl=sl: e.dma_start(out=xst[sl][:], in_=x_h[t * 128:(t + 1) * 128, :]),
           waits=[(s_xfree[sl], max(0, t // 2))], inc=s_xld[sl], inc_n=16)
        st("vector", lambda e, sl=sl: e.reduce_sum(lnrows[:, 0:1], xst[sl][:], axis=mybir.AxisListType.X),
           waits=[(s_xld[sl], s_xld[sl].n)], inc=s_ln)
        v0 = s_ln.n
        st("scalar", lambda e, sl=sl: e.activation(out=sqscr[:], in_=xst[sl][:], func=AF.Square,
                                                   accum_out=lnrows[:, 1:2]),
           waits=[(s_xld[sl], s_xld[sl].n), (s_ln, v0), (s_misc, SETUP_DMA)], inc=s_lna)
        a0 = s_lna.n
        st("vector", lambda e: e.tensor_scalar(out=lnrows[:, 2:3], in0=lnrows[:, 0:1],
                                               scalar1=1.0 / 1024.0, scalar2=None, op0=ALU.mult),
           waits=[(s_lna, a0)], inc=s_ln)
        st("vector", lambda e: e.tensor_scalar(out=lnrows[:, 3:4], in0=lnrows[:, 1:2],
                                               scalar1=1.0 / 1024.0, scalar2=None, op0=ALU.mult),
           waits=[(s_ln, s_ln.n)], inc=s_ln)
        st("vector", lambda e: e.tensor_tensor(out=lnrows[:, 4:5], in0=lnrows[:, 2:3],
                                               in1=lnrows[:, 2:3], op=ALU.mult),
           waits=[(s_ln, s_ln.n)], inc=s_ln)
        st("vector", lambda e: e.tensor_tensor(out=lnrows[:, 5:6], in0=lnrows[:, 3:4],
                                               in1=lnrows[:, 4:5], op=ALU.subtract),
           waits=[(s_ln, s_ln.n)], inc=s_ln)
        v1 = s_ln.n
        st("scalar", lambda e: e.activation(out=lnrows[:, 6:7], in_=lnrows[:, 5:6], func=AF.Ln,
                                            bias=LNEPS),
           waits=[(s_ln, v1), (s_lna, s_lna.n)], inc=s_lna)
        st("scalar", lambda e: e.activation(out=lnrows[:, 7:8], in_=lnrows[:, 6:7], func=AF.Exp,
                                            scale=-0.5),
           waits=[(s_lna, s_lna.n)], inc=s_lna)
        a1 = s_lna.n
        st("vector", lambda e, sl=sl: e.tensor_scalar(out=xnf[sl][:], in0=xst[sl][:],
                                                      scalar1=lnrows[:, 2:3], scalar2=lnrows[:, 7:8],
                                                      op0=ALU.subtract, op1=ALU.mult),
           waits=[(s_lna, a1), (s_pt, max(0, s_pt.n - 8))], inc=s_ln)
        st("vector", lambda e: e.nop(), waits=[(s_ln, s_ln.n)], inc=s_xfree[sl])
        vn = s_ln.n
        for half in range(2):
            trs = []
            for kk in range(4):
                k = half * 4 + kk

                def tf(e, ptt, sl=sl, k=k, kk=kk):
                    return e.transpose(ptt[:, kk * 128:(kk + 1) * 128],
                                       xnf[sl][:, k * 128:(k + 1) * 128], identb[:])
                trs.append(tf)

            def ev(e, ptt, t=t, half=half):
                dst = XnT3[:, half * 4:(half + 1) * 4, t * 128:(t + 1) * 128]
                src = ptt[:, 0:512].rearrange("p (a b) -> p a b", b=128)
                return e.tensor_copy(out=dst, in_=src)
            w0 = [(s_ln, vn)]
            if t == 0 and half == 0:
                w0.append((s_misc, SETUP_DMA))
            pt_cycle(trs, ev, pe_waits=w0)
    st("vector", lambda e: e.nop(), waits=[(s_ptf, s_ptf.n)], inc=s_xnt)
    LN1_XNT = s_xnt.n
    sfree("xnf1", "xnf0")
    sfree("pT1", "pT0")
    pAb = [palloc("pA", [128, 512], F32, side="right"),
           palloc("pA2", [128, 512], F32, side="right")]

    H_ec = [salloc(f"H{i}", [128, TOWN], BF16, side="right") for i in range(8)]
    Kaug = [salloc(f"Kaug{i}", [66, SLEN], BF16, side="right") for i in range(2)]
    Qaug = [salloc(f"Qaug{i}", [66, TOWN], BF16, side="right") for i in range(2)]
    Vs = [salloc(f"Vs{i}", [128, NST * 65], BF16, side="right") for i in range(2)]
    Gt = [salloc(f"Gt{i}", [64, TOWN], BF16, side="right") for i in range(2)]
    Wst = [salloc(f"Wst{i}", [128, 256], F32, side="right") for i in range(2)]
    Wbf = [salloc(f"Wbf{i}", [128, 2048], BF16, side="right") for i in range(2)]
    ebv = salloc("ebv", [128, 16], F32, side="right")
    KQsq = [salloc(f"KQsq{i}", [64, 512], BF16, side="right") for i in range(2)]
    k2row = [salloc(f"k2row{i}", [1, 512], BF16, side="right") for i in range(2)]
    Lb = [salloc(f"Lb{i}", [128, 1024], F32, side="right") for i in range(2)]
    Eb = [salloc(f"Eb{i}", [128, 1024], F32, side="right") for i in range(2)]
    wb = [salloc(f"wb{i}", [128, 1024], BF16, side="right") for i in range(4)]
    A_sb = [salloc(f"A_sb{i}", [65, 512], F32, side="right") for i in range(2)]
    rsrow = salloc("rsrow", [1, 512], F32, side="right")
    rlrow = salloc("rlrow", [1, 512], F32, side="right")
    rnrow = salloc("rnrow", [1, 512], F32, side="right")
    rnhl = salloc("rnhl", [1, 1024], BF16, side="right")
    T1b = salloc("T1b", [64, 512], F32, side="right")
    Htmp = salloc("Htmp", [64, 512], BF16, side="right")

    st("vector", lambda e: e.memset(ebv[:], 0.0), inc=s_mset)
    st("vector", lambda e: e.tensor_scalar(
        out=ebv[0:64, :].rearrange("p (h o) -> p h o", o=1),
        in0=bin_sb[0:64, :].rearrange("p (h t) -> p h t", t=2)[:, :, 1:2],
        scalar1=EPS, scalar2=None, op0=ALU.mult),
       waits=[(s_mset, s_mset.n), (s_misc, SETUP_DMA)], inc=s_mset)
    st("vector", lambda e: e.memset(ones_col[:], 1.0), inc=s_mset)
    st("vector", lambda e: e.memset(q25_col[:], 0.25), inc=s_mset)
    st("vector", lambda e: e.memset(i1024_col[:], 1.0 / 1024.0), inc=s_mset)
    st("vector", lambda e: e.memset(ones_rowb[:], 1.0), inc=s_mset)
    st("vector", lambda e: e.memset(Vs[0][:], 1.0), inc=s_mset)
    st("vector", lambda e: e.memset(Vs[1][:], 1.0), inc=s_mset)
    st("vector", lambda e: e.memset(Kaug[0][64:65, :], 1.0), inc=s_mset)
    st("vector", lambda e: e.memset(Kaug[1][64:65, :], 1.0), inc=s_mset)
    MSET_ALL = s_mset.n
    for i in range(2):
        for n in range(NQC):
            w = [(s_mset, MSET_ALL)] if (i == 0 and n == 0) else []
            st("gpsimd", lambda e, i=i, n=n: e.dma_start(
                out=Qaug[i][65:66, n * 512:(n + 1) * 512], in_=ones_rowb[:]),
               waits=w, inc=s_aug1, inc_n=16)
    AUG1 = s_aug1.n

    pp_seq = [0]

    def pp_cycle(pe_fns, evac_fn, pe_waits=(), ev_waits=(), ev_inc=None):
        pp_seq[0] += 1
        for i, f in enumerate(pe_fns):
            w = list(pe_waits) if i == 0 else []
            if i == 0:
                w.append((s_ppf, pp_seq[0] - 1))
            st("tensor", (lambda e, f=f: f(e)), waits=w, inc=s_pp)
        st("vector", (lambda e, f=evac_fn: f(e)),
           waits=[(s_pp, s_pp.n)] + list(ev_waits), inc=s_ppf)
        tgt = s_ppf.n
        if ev_inc is not None:
            st("vector", lambda e: e.nop(), waits=[(s_ppf, tgt)], inc=ev_inc)
        return tgt

    vt_use = [0]
    sq_use = [0]

    def proj_head_w(h):
        hb = h % 2
        st("vector", lambda e: e.nop(), waits=[(s_hbf[hb], h // 2)])
        for c in range(NCH):
            sl = c % 2
            uses = h * NCH + c
            st("sync", lambda e, h=h, c=c, sl=sl: e.dma_start(
                out=Wst[sl][:], in_=win_h[c * 128:(c + 1) * 128, h * 256:(h + 1) * 256]),
               waits=[(s_wfree[sl], max(0, (uses - 2) // 2 + 1 if uses >= 2 else 0))],
               inc=s_wld[sl], inc_n=16)
            st("vector", lambda e, c=c, sl=sl, hb=hb: e.tensor_copy(
                out=Wbf[hb][:, c * 256:(c + 1) * 256], in_=Wst[sl][:]),
               waits=[(s_wld[sl], s_wld[sl].n)], inc=s_wcast)
            st("vector", lambda e, sl=sl: e.nop(), waits=[(s_wcast, s_wcast.n)], inc=s_wfree[sl])
        return s_wcast.n

    def proj_head(h, wc_done):
        hb = h % 2

        def family(cols0, nchunks_t, evac):
            for n in range(nchunks_t):
                grp()
                fns = []
                for c in range(NCH):
                    def f(e, c=c, n=n, cols0=cols0, hb=hb):
                        return e.matmul(
                            pp[0:64, :],
                            Wbf[hb][:, c * 256 + cols0:c * 256 + cols0 + 64],
                            XnT[:, c * SLEN + n * 512:c * SLEN + (n + 1) * 512],
                            start=(c == 0), stop=(c == NCH - 1))
                    fns.append(f)
                pp_cycle(fns, evac(n), pe_waits=[(s_wcast, wc_done), (s_xnt, LN1_XNT)])

        def q_ev(n):
            def f(e, n=n, hb=hb, h=h):
                return e.tensor_scalar(out=Qaug[hb][0:64, n * 512:(n + 1) * 512],
                                       in0=pp[0:64, :],
                                       scalar1=bin_sb[0:64, 2 * h:2 * h + 1], scalar2=-2.0,
                                       op0=ALU.add, op1=ALU.mult)
            return f
        family(0, NQC, q_ev)

        def k_ev(n):
            def f(e, n=n, hb=hb, h=h):
                return e.tensor_scalar(out=Kaug[hb][0:64, n * 512:(n + 1) * 512],
                                       in0=pp[0:64, :],
                                       scalar1=bin_sb[64:128, 2 * h:2 * h + 1], scalar2=None,
                                       op0=ALU.add)
            return f
        family(64, NST // 4, k_ev)

        for t in range(NST):
            grp()
            fns = []
            for c in range(NCH):
                def f(e, c=c, t=t, hb=hb):
                    return e.matmul(pp[0:128, 0:64],
                                    XnT[:, c * SLEN + t * 128:c * SLEN + (t + 1) * 128],
                                    Wbf[hb][:, c * 256 + 128:c * 256 + 192],
                                    start=(c == 0), stop=(c == NCH - 1))
                fns.append(f)

            def v_ev(e, t=t, hb=hb):
                return e.tensor_copy(out=Vs[hb][:, t * 65:t * 65 + 64], in_=pp[0:128, 0:64])
            pp_cycle(fns, v_ev, pe_waits=[(s_wcast, wc_done), (s_xnt, LN1_XNT)])

        def g_ev(n):
            def f(e, n=n, hb=hb, h=h):
                return e.tensor_scalar(out=Gt[hb][:, n * 512:(n + 1) * 512],
                                       in0=pp[0:64, :],
                                       scalar1=bin_sb[64:128, 2 * h + 1:2 * h + 2], scalar2=None,
                                       op0=ALU.add)
            return f
        family(192, NQC, g_ev)

        for kind, srcb, nn, scol in (("k2", Kaug[hb], NST // 4, ones_col),
                                     ("q2", Qaug[hb], NQC, q25_col)):
            for n in range(nn):
                grp()
                sq_use[0] += 1
                myuse = sq_use[0]
                sl = myuse % 2
                st("vector", lambda e, srcb=srcb, n=n, sl=sl: e.tensor_tensor(
                    out=KQsq[sl][:], in0=srcb[0:64, n * 512:(n + 1) * 512],
                    in1=srcb[0:64, n * 512:(n + 1) * 512], op=ALU.mult),
                   waits=[(s_sqf[sl], max(0, (myuse - 2 - 1) // 2 + 1 if myuse > 2 else 0))],
                   inc=s_sqt)
                sqtn = s_sqt.n

                def mm(e, sl=sl, scol=scol):
                    return e.matmul(pp[0:1, :], scol[0:64, :], KQsq[sl][:],
                                    start=True, stop=True)

                def ev(e, sl=sl):
                    return e.tensor_copy(out=k2row[sl][:], in_=pp[0:1, :])
                pp_cycle([mm], ev, pe_waits=[(s_sqt, sqtn)],
                         ev_waits=[(s_k2dma[sl], s_k2dma[sl].n)], ev_inc=s_sqf[sl])
                k2t = s_sqf[sl].n
                if kind == "k2":
                    st("gpsimd", lambda e, hb=hb, n=n, sl=sl: e.dma_start(
                        out=Kaug[hb][65:66, n * 512:(n + 1) * 512], in_=k2row[sl][:]),
                       waits=[(s_sqf[sl], k2t), (s_hbf[hb], h // 2), (s_aug1, AUG1)],
                       inc=s_k2dma[sl], inc_n=16)
                else:
                    st("gpsimd", lambda e, hb=hb, n=n, sl=sl: e.dma_start(
                        out=Qaug[hb][64:65, n * 512:(n + 1) * 512], in_=k2row[sl][:]),
                       waits=[(s_sqf[sl], k2t), (s_hbf[hb], h // 2)],
                       inc=s_k2dma[sl], inc_n=16)

        st("vector", lambda e: e.nop(), waits=[(s_ppf, s_ppf.n), (s_ptf, s_ptf.n)], inc=s_prj)
        st("gpsimd", lambda e: e.nop(),
           waits=[(s_k2dma[0], s_k2dma[0].n), (s_k2dma[1], s_k2dma[1].n), (s_prj, s_prj.n)],
           inc=s_hb[hb])
        return s_hb[hb].n

    NTOT = HEADS * NQC * NREG

    def rinfo(r):
        return r // (NQC * NREG), (r // NREG) % NQC, r % NREG

    h_ready = [0, 0]

    staging_groups = None
    stagingW = None
    moved_groups = [0]

    def grp():
        if staging_groups is not None:
            staging_groups.append({e: [] for e in ENGINES})
            cur_steps[0] = staging_groups[-1]

    def stage_proj(h):
        nonlocal staging_groups, stagingW
        stagingW = {e: [] for e in ENGINES}
        cur_steps[0] = stagingW
        wc = proj_head_w(h)
        staging_groups = [{e: [] for e in ENGINES}]
        cur_steps[0] = staging_groups[-1]
        hr = proj_head(h, wc)
        cur_steps[0] = steps
        moved_groups[0] = 0
        return hr

    def flush_w():
        nonlocal stagingW
        if stagingW is None:
            return
        for e in ENGINES:
            steps[e].extend(stagingW[e])
        stagingW = None

    def splice(frac_num, frac_den):
        if staging_groups is None:
            return
        target = len(staging_groups) * frac_num // frac_den
        for g in staging_groups[moved_groups[0]:target]:
            for e in ("tensor", "vector", "scalar", "gpsimd", "sync"):
                steps[e].extend(g[e])
        moved_groups[0] = max(moved_groups[0], target)

    h_ready[0] = proj_head(0, proj_head_w(0))
    prev_e4b = [0, 0]

    def emit_S(r):
        h, qc, g = rinfo(r)
        hb = h % 2
        ps = r % 2
        for k in range(2):
            stile = 2 * g + k
            w = [(s_hb[hb], h_ready[hb])]
            if k == 0 and r >= 2:
                w.append((s_c1, r - 1))
            st("tensor", lambda e, hb=hb, qc=qc, stile=stile, ps=ps, k=k: e.matmul(
                pS[ps][:, k * 512:(k + 1) * 512],
                Kaug[hb][:, stile * 128:(stile + 1) * 128],
                Qaug[hb][:, qc * 512:(qc + 1) * 512],
                start=True, stop=True),
               waits=w, inc=s_peS)

    def emit_c12(r):
        ps = r % 2
        w = [(s_peS, 2 * r + 2)]
        if r >= 2:
            w.append((s_c3, r - 1))
        st("scalar", lambda e, ps=ps, r=r: e.activation(
            out=Lb[r % 2][:], in_=pS[ps][:], func=AF.Ln), waits=w, inc=s_c1)
        w = [(s_c1, r + 1)]
        if r >= 2:
            w.append((s_c4, r - 1))
        st("scalar", lambda e, r=r: e.activation(
            out=Eb[r % 2][:], in_=Lb[r % 2][:], func=AF.Exp, scale=-0.5, bias=LN2EPS),
           waits=w, inc=s_c2)

    def emit_c3(r):
        st("vector", lambda e, r=r: e.tensor_tensor(
            out=Eb[r % 2][:], in0=Lb[r % 2][:], in1=Eb[r % 2][:], op=ALU.add),
           waits=[(s_c2, r + 1)], inc=s_c3)

    def emit_c4(r):
        w = [(s_c3, r + 1)]
        if r >= 4:
            w.append((s_wcons, 2 * r - 6))
        st("scalar", lambda e, r=r, ws=r % 4: e.activation(
            out=wb[ws][:], in_=Eb[r % 2][:], func=AF.Exp, scale=-1.0),
           waits=w, inc=s_c4)

    def emit_A(r):
        h, qc, g = rinfo(r)
        hb = h % 2
        bk = (h * NQC + qc) % 2
        for k in range(2):
            stile = 2 * g + k
            w = [(s_c4, r + 1)]
            if stile == 0:
                w.append((s_epi, prev_e4b[bk]))
            st("tensor", lambda e, hb=hb, stile=stile, ws=r % 4, k=k, bk=bk: e.matmul(
                pAb[bk][0:65, :],
                Vs[hb][:, stile * 65:stile * 65 + 65],
                wb[ws][:, k * 512:(k + 1) * 512],
                start=(stile == 0), stop=(stile == NST - 1)),
               waits=w, inc=s_wcons)

    def emit_epilogue(r_end):
        h, qc, g = rinfo(r_end)
        hb = h % 2
        n_epi = h * NQC + qc
        asl = n_epi % 2
        bk = n_epi % 2
        st("vector", lambda e, asl=asl, bk=bk, h=h: e.tensor_scalar(
            out=A_sb[asl][:], in0=pAb[bk][0:65, :],
            scalar1=ebv[0:65, h:h + 1], scalar2=None, op0=ALU.subtract),
           waits=[(s_wcons, 2 * (r_end + 1)), (s_epi, max(0, s_epi.n - 5)),
                  (s_rsd, max(0, s_rsd.n - 16))], inc=s_e1)
        e1n = s_e1.n
        st("gpsimd", lambda e, asl=asl: e.dma_start(out=rsrow[:], in_=A_sb[asl][64:65, :]),
           waits=[(s_e1, e1n), (s_rn, s_rn.n)], inc=s_rsd, inc_n=16)
        rsn = s_rsd.n
        st("scalar", lambda e: e.activation(out=rlrow[:], in_=rsrow[:], func=AF.Ln, bias=EPS),
           waits=[(s_rsd, rsn)], inc=s_rn)
        st("scalar", lambda e: e.activation(out=rnrow[:], in_=rlrow[:], func=AF.Exp, scale=-1.0),
           waits=[(s_rn, s_rn.n)], inc=s_rn)
        rnn = s_rn.n
        st("vector", lambda e: e.tensor_copy(out=rnhl[:, 0:512], in_=rnrow[:]),
           waits=[(s_rn, rnn), (s_bc, s_bc.n)], inc=s_epi)
        st("vector", lambda e: e.tensor_tensor(out=rnhl[:, 512:1024], in0=rnrow[:],
                                               in1=rnhl[:, 0:512], op=ALU.subtract),
           waits=[(s_epi, s_epi.n)], inc=s_epi)
        spl = s_epi.n
        w = [(s_epi, spl), (s_e1, e1n)]
        if n_epi == 0:
            w.append((s_xnt, LN1_XNT))
        st("tensor", lambda e, bk=bk: e.matmul(pAb[bk][0:64, :], ones_rowb[:, 0:64],
                                                rnhl[:, 0:512],
                                                start=True, stop=False),
           waits=w, inc=s_bc)
        st("tensor", lambda e, bk=bk: e.matmul(pAb[bk][0:64, :], ones_rowb[:, 0:64],
                                                rnhl[:, 512:1024],
                                                start=False, stop=True),
           waits=[], inc=s_bc)
        bcn = s_bc.n
        st("vector", lambda e, asl=asl, bk=bk: e.tensor_tensor(out=T1b[:],
                                                               in0=A_sb[asl][0:64, :],
                                                               in1=pAb[bk][0:64, :],
                                                               op=ALU.mult),
           waits=[(s_bc, bcn)], inc=s_epi)
        prev_e4b[bk] = s_epi.n
        ec = h // 2
        if h % 2 == 0:
            st("vector", lambda e, hb=hb, qc=qc, ec=ec, h=h: e.scalar_tensor_tensor(
                out=H_ec[ec][0:64, qc * 512:(qc + 1) * 512],
                in0=T1b[:], scalar=bin_sb[0:64, 2 * h + 1:2 * h + 2],
                in1=Gt[hb][:, qc * 512:(qc + 1) * 512],
                op0=ALU.add, op1=ALU.mult),
               waits=[(s_epi, s_epi.n)], inc=s_epi)
        else:
            st("vector", lambda e, hb=hb, qc=qc, h=h: e.scalar_tensor_tensor(
                out=Htmp[:], in0=T1b[:], scalar=bin_sb[0:64, 2 * h + 1:2 * h + 2],
                in1=Gt[hb][:, qc * 512:(qc + 1) * 512],
                op0=ALU.add, op1=ALU.mult),
               waits=[(s_epi, s_epi.n), (s_hdma, s_hdma.n)], inc=s_epi)
            epn = s_epi.n
            st("gpsimd", lambda e, qc=qc, ec=ec: e.dma_start(
                out=H_ec[ec][64:128, qc * 512:(qc + 1) * 512], in_=Htmp[:]),
               waits=[(s_epi, epn)], inc=s_hdma, inc_n=16)
        if qc == NQC - 1:
            st("vector", lambda e: e.nop(), waits=[(s_epi, s_epi.n)], inc=s_hbf[hb])

    emit_S(0)
    for r in range(NTOT):
        h, qc, g = rinfo(r)
        if g == 0 and qc == 0:
            if h + 1 < HEADS:
                h_ready[(h + 1) % 2] = stage_proj(h + 1)
            else:
                staging_groups = None
        emit_c12(r)
        emit_c3(r)
        nxt_head_start = (r + 1) % (NQC * NREG) == 0
        if r + 1 < NTOT and not nxt_head_start:
            emit_S(r + 1)
        if r >= 1:
            emit_c4(r - 1)
            emit_A(r - 1)
            hh, qq, gg = rinfo(r - 1)
            if gg == NREG - 1:
                emit_epilogue(r - 1)
        slot_in_head = r % (NQC * NREG)
        if slot_in_head == 0:
            flush_w()
        splice(slot_in_head + 1, NQC * NREG)
        if r + 1 < NTOT and nxt_head_start:
            emit_S(r + 1)
    emit_c4(NTOT - 1)
    emit_A(NTOT - 1)
    emit_epilogue(NTOT - 1)

    ATT_EPI = s_epi.n
    ATT_HDMA = s_hdma.n
    ATT_E1 = s_e1.n
    CH_C1 = s_c1.n

    sfree("Htmp", "T1b", "rnhl", "rnrow", "rlrow", "rsrow", "A_sb1", "A_sb0",
          "wb3", "wb2", "wb1", "wb0", "Eb1", "Eb0", "Lb1", "Lb0",
          "k2row1", "k2row0", "KQsq1", "KQsq0", "ebv",
          "Wbf1", "Wbf0", "Wst1", "Wst0", "Gt1", "Gt0", "Vs1", "Vs0",
          "Qaug1", "Qaug0", "Kaug1", "Kaug0")
    sfree("XnT")
    sfree("pA2", "pA")

    Wout_bf = salloc("Wout_bf", [128, 8 * D], BF16, side="right")
    XT = salloc("XT", [128, NCH * TOWN], F32)
    x1T = salloc("x1T", [128, NCH * TOWN], F32)

    for c in range(NCH):
        sl = c % 2
        st("sync", lambda e, c=c, sl=sl: e.dma_start(out=xst[sl][:], in_=wout_h[c * 128:(c + 1) * 128, :]),
           waits=[(s_xfree[sl], NST // 2), (s_wo, max(0, s_wo.n - 1))], inc=s_xld[sl], inc_n=16)
        st("vector", lambda e, c=c, sl=sl: e.tensor_copy(out=Wout_bf[:, c * 1024:(c + 1) * 1024],
                                                         in_=xst[sl][:]),
           waits=[(s_xld[sl], s_xld[sl].n), (s_epi, ATT_EPI)], inc=s_wo)
    WO_DONE = s_wo.n
    for c in range(NCH):
        st("sync", lambda e, c=c: e.dma_start(out=XT[:, c * TOWN:(c + 1) * TOWN],
                                              in_=xT_h[c * 128:(c + 1) * 128, :]),
           waits=[], inc=s_xt, inc_n=16)
    XT_DONE = s_xt.n

    for dt in range(NCH):
        for tc in range(NQC):
            slot = (dt * NQC + tc) % 4
            pso = pS[slot // 2]
            off = (slot % 2) * 512
            for ec in range(NCH):
                w = []
                if ec == 0:
                    w = [(s_wo, WO_DONE), (s_hdma, ATT_HDMA), (s_epi, ATT_EPI),
                         (s_c1, CH_C1), (s_x1, max(0, s_x1.n - 3))]
                st("tensor", lambda e, dt=dt, tc=tc, ec=ec, pso=pso, off=off: e.matmul(
                    pso[:, off:off + 512],
                    Wout_bf[:, ec * 1024 + dt * 128:ec * 1024 + dt * 128 + 128],
                    H_ec[ec][:, tc * 512:(tc + 1) * 512],
                    start=(ec == 0), stop=(ec == NCH - 1)),
                   waits=w, inc=s_op)
            opn = s_op.n
            st("vector", lambda e, dt=dt, tc=tc, pso=pso, off=off: e.scalar_tensor_tensor(
                out=x1T[:, dt * TOWN + tc * 512:dt * TOWN + (tc + 1) * 512],
                in0=pso[:, off:off + 512], scalar=bout_sb[:, dt:dt + 1],
                in1=XT[:, dt * TOWN + tc * 512:dt * TOWN + (tc + 1) * 512],
                op0=ALU.add, op1=ALU.add),
               waits=[(s_op, opn), (s_xt, XT_DONE)], inc=s_x1)
    X1_DONE = s_x1.n

    sfree("Wout_bf")
    sfree("H7", "H6", "H5", "H4", "H3", "H2", "H1", "H0")
    signWT = salloc("signWT", [128, 8 * D], BF16, side="right")
    x1n_bf = salloc("x1n_bf", [128, NCH * TOWN], BF16, side="right")
    sgb = salloc("sgb", [128, D], BF16, side="right")
    sqb = salloc("sqb", [128, 512], BF16, side="right")
    srows = salloc("srows", [1, 2560], F32, side="right")
    srow = {"m": srows[:, 0:512], "q": srows[:, 512:1024], "mm": srows[:, 1024:1536],
            "var": srows[:, 1536:2048], "lv": srows[:, 1024:1536],
            "rstd": srows[:, 2048:2560], "b": srows[:, 512:1024]}
    abhl = salloc("abhl", [1, 2048], BF16, side="right")

    pT = [palloc(f"pTs{i}", [128, 512], BF16, side="right") for i in range(2)]
    sWT3 = signWT[:].rearrange("p (c o) -> p c o", o=D)
    for oc in range(NCH):
        sl = oc % 2
        st("sync", lambda e, oc=oc, sl=sl: e.dma_start(out=xst[sl][:], in_=wffn_h[oc * 128:(oc + 1) * 128, :]),
           waits=[(s_sg, max(0, s_sg.n - 1)), (s_wo, WO_DONE)], inc=s_xld[sl], inc_n=16)
        st("scalar", lambda e, sl=sl: e.activation(out=sgb[:], in_=xst[sl][:], func=AF.Sign),
           waits=[(s_xld[sl], s_xld[sl].n), (s_ptf, s_ptf.n),
                  (s_x1, X1_DONE)], inc=s_sg)
        sgn = s_sg.n
        for half in range(2):
            trs = []
            for kk in range(4):
                k = half * 4 + kk

                def tf(e, ptt, k=k, kk=kk):
                    return e.transpose(ptt[:, kk * 128:(kk + 1) * 128],
                                       sgb[:, k * 128:(k + 1) * 128], identb[:])
                trs.append(tf)

            def ev(e, ptt, oc=oc, half=half):
                dst = sWT3[:, half * 4:(half + 1) * 4, oc * 128:(oc + 1) * 128]
                src = ptt[:, 0:512].rearrange("p (a b) -> p a b", b=128)
                return e.tensor_copy(out=dst, in_=src)
            pt_cycle(trs, ev, pe_waits=[(s_sg, sgn)])
    st("vector", lambda e: e.nop(), waits=[(s_ptf, s_ptf.n)], inc=s_sg)
    SG_DONE = s_sg.n
    sfree("pTs1", "pTs0")
    pA_ln2 = palloc("pAl", [128, 512], F32, side="right")

    for tc in range(NQC):
        for c in range(NCH):
            a0 = c * TOWN + tc * 512
            a1_ = c * TOWN + (tc + 1) * 512
            st("vector", lambda e, a0=a0, a1_=a1_: e.tensor_copy(
                out=x1n_bf[:, a0:a1_], in_=x1T[:, a0:a1_]),
               waits=[(s_x1, X1_DONE)], inc=s_xbf)
            xbn = s_xbf.n
            w0 = [(s_xbf, xbn)]
            if c == 0:
                w0 += [(s_str, s_str.n), (s_abf, s_abf.n)]
                if tc == 0:
                    w0 += [(s_e1, ATT_E1), (s_epi, ATT_EPI), (s_fp, 0),
                           (s_x1, X1_DONE)]
            st("tensor", lambda e, a0=a0, a1_=a1_, c=c: e.matmul(
                pA_ln2[0:1, :], i1024_col[:], x1n_bf[:, a0:a1_],
                start=(c == 0), stop=(c == NCH - 1)),
               waits=[wi for wi in w0 if wi[1] > 0], inc=s_st1)
            st("vector", lambda e, a0=a0, a1_=a1_: e.tensor_tensor(
                out=sqb[:], in0=x1n_bf[:, a0:a1_], in1=x1n_bf[:, a0:a1_], op=ALU.mult),
               waits=[(s_xbf, xbn), (s_st1, s_st1.n)], inc=s_xbf)
            st("tensor", lambda e, c=c: e.matmul(
                pS[1][0:1, 0:512], i1024_col[:], sqb[:],
                start=(c == 0), stop=(c == NCH - 1)),
               waits=[(s_xbf, s_xbf.n)], inc=s_st1)
        stn = s_st1.n
        st("vector", lambda e: e.tensor_copy(out=srow["m"], in_=pA_ln2[0:1, :]),
           waits=[(s_st1, stn)], inc=s_str)
        st("vector", lambda e: e.tensor_copy(out=srow["q"], in_=pS[1][0:1, 0:512]),
           waits=[(s_str, s_str.n)], inc=s_str)
        st("vector", lambda e: e.tensor_tensor(out=srow["mm"], in0=srow["m"], in1=srow["m"],
                                               op=ALU.mult),
           waits=[(s_str, s_str.n)], inc=s_str)
        st("vector", lambda e: e.tensor_tensor(out=srow["var"], in0=srow["q"], in1=srow["mm"],
                                               op=ALU.subtract),
           waits=[(s_str, s_str.n)], inc=s_str)
        vrn = s_str.n
        st("scalar", lambda e: e.activation(out=srow["lv"], in_=srow["var"], func=AF.Ln,
                                            bias=LNEPS),
           waits=[(s_str, vrn), (s_sta, s_sta.n)], inc=s_sta)
        st("scalar", lambda e: e.activation(out=srow["rstd"], in_=srow["lv"], func=AF.Exp,
                                            scale=-0.5),
           waits=[(s_sta, s_sta.n)], inc=s_sta)
        san = s_sta.n
        st("vector", lambda e: e.scalar_tensor_tensor(out=srow["b"], in0=srow["m"],
                                                      scalar=-1.0, in1=srow["rstd"],
                                                      op0=ALU.mult, op1=ALU.mult),
           waits=[(s_sta, san), (s_str, s_str.n)], inc=s_str)
        st("vector", lambda e: e.tensor_copy(out=abhl[:, 0:512], in_=srow["rstd"]),
           waits=[(s_str, s_str.n), (s_abf, s_abf.n)], inc=s_str)
        st("vector", lambda e: e.tensor_tensor(out=abhl[:, 512:1024], in0=srow["rstd"],
                                               in1=abhl[:, 0:512], op=ALU.subtract),
           waits=[(s_str, s_str.n)], inc=s_str)
        st("vector", lambda e: e.tensor_copy(out=abhl[:, 1024:1536], in_=srow["b"]),
           waits=[(s_str, s_str.n)], inc=s_str)
        st("vector", lambda e: e.tensor_tensor(out=abhl[:, 1536:2048], in0=srow["b"],
                                               in1=abhl[:, 1024:1536], op=ALU.subtract),
           waits=[(s_str, s_str.n)], inc=s_str)
        brn = s_str.n
        st("tensor", lambda e: e.matmul(pA_ln2[:, 0:512], ones_rowb[:, 0:128], abhl[:, 0:512],
                                        start=True, stop=False),
           waits=[(s_str, brn)], inc=s_ab)
        st("tensor", lambda e: e.matmul(pA_ln2[:, 0:512], ones_rowb[:, 0:128], abhl[:, 512:1024],
                                        start=False, stop=True),
           waits=[], inc=s_ab)
        st("tensor", lambda e: e.matmul(pS[0][:, 0:512], ones_rowb[:, 0:128], abhl[:, 1024:1536],
                                        start=True, stop=False),
           waits=[], inc=s_ab)
        st("tensor", lambda e: e.matmul(pS[0][:, 0:512], ones_rowb[:, 0:128], abhl[:, 1536:2048],
                                        start=False, stop=True),
           waits=[], inc=s_ab)
        abn = s_ab.n
        for c in range(NCH):
            a0 = c * TOWN + tc * 512
            a1_ = c * TOWN + (tc + 1) * 512
            st("vector", lambda e, a0=a0, a1_=a1_: e.tensor_tensor(
                out=sqscr[:, 0:512], in0=x1T[:, a0:a1_], in1=pA_ln2[:, 0:512], op=ALU.mult),
               waits=[(s_ab, abn), (s_x1n, s_x1n.n)], inc=s_x1n)
            st("vector", lambda e, a0=a0, a1_=a1_: e.tensor_tensor(
                out=x1n_bf[:, a0:a1_], in0=sqscr[:, 0:512], in1=pS[0][:, 0:512], op=ALU.add),
               waits=[(s_x1n, s_x1n.n)], inc=s_x1n)
        st("vector", lambda e: e.nop(), waits=[(s_x1n, s_x1n.n)], inc=s_abf)
    X1N_DONE = s_x1n.n

    x2T = XT
    for ot in range(NCH):
        for tc in range(NQC):
            slot = (ot * NQC + tc) % 4
            pso = pS[slot // 2]
            off = (slot % 2) * 512
            for ic in range(NCH):
                w = []
                if ic == 0:
                    w = [(s_sg, SG_DONE), (s_x1n, X1N_DONE), (s_op, s_op.n),
                         (s_x2, max(0, s_x2.n - 3)), (s_abf, s_abf.n)]
                st("tensor", lambda e, ot=ot, tc=tc, ic=ic, pso=pso, off=off: e.matmul(
                    pso[:, off:off + 512],
                    signWT[:, ic * 1024 + ot * 128:ic * 1024 + ot * 128 + 128],
                    x1n_bf[:, ic * TOWN + tc * 512:ic * TOWN + (tc + 1) * 512],
                    start=(ic == 0), stop=(ic == NCH - 1)),
                   waits=w, inc=s_fp)
            fpn = s_fp.n
            st("vector", lambda e, ot=ot, tc=tc, pso=pso, off=off: e.tensor_tensor(
                out=x2T[:, ot * TOWN + tc * 512:ot * TOWN + (tc + 1) * 512],
                in0=pso[:, off:off + 512],
                in1=x1T[:, ot * TOWN + tc * 512:ot * TOWN + (tc + 1) * 512], op=ALU.add),
               waits=[(s_fp, fpn), (s_x1, X1_DONE), (s_xt, XT_DONE)], inc=s_x2)
            x2n = s_x2.n
            st("sync", lambda e, ot=ot, tc=tc: e.dma_start(
                out=y_h[ot * 128:(ot + 1) * 128, tc * 512:(tc + 1) * 512],
                in_=x2T[:, ot * TOWN + tc * 512:ot * TOWN + (tc + 1) * 512]),
               waits=[(s_x2, x2n)], inc=s_store, inc_n=16)
    st("sync", lambda e: e.nop(), waits=[(s_store, s_store.n)])

    with contextlib.ExitStack() as es2:
        for s in sems:
            s.h = es2.enter_context(nc.semaphore(s.name))
        block = es2.enter_context(nc.Block())

        def emitter(name):
            def run(eng):
                for waits, fn, inc, inc_n in steps[name]:
                    for (sm, val) in waits:
                        if val > 0:
                            eng.wait_ge(sm.h, val)
                    inst = fn(eng)
                    if inc is not None:
                        inst.then_inc(inc.h, inc_n)
            return run

        block.sync(emitter("sync"))
        block.tensor(emitter("tensor"))
        block.vector(emitter("vector"))
        block.scalar(emitter("scalar"))
        block.gpsimd(emitter("gpsimd"))

    for g in reversed(list(guards.values())):
        g.__exit__(None, None, None)
    guards.clear()
    return nc


_NC_CACHE = None


def kernel(L0_attn_X, L0_attn_W_in, L0_attn_b_in, L0_attn_W_out, L0_attn_b_out, L0_ffn_weight):
    global _NC_CACHE
    X = np.asarray(L0_attn_X, dtype=np.float32)
    W_in = np.ascontiguousarray(np.asarray(L0_attn_W_in, dtype=np.float32))
    b_in = np.asarray(L0_attn_b_in, dtype=np.float32)
    W_out = np.ascontiguousarray(np.asarray(L0_attn_W_out, dtype=np.float32))
    b_out = np.asarray(L0_attn_b_out, dtype=np.float32)
    W_ffn = np.ascontiguousarray(np.asarray(L0_ffn_weight, dtype=np.float32))

    import ml_dtypes
    bin_t = np.ascontiguousarray(b_in.reshape(32, 128).T)
    bout_t = np.ascontiguousarray(b_out.reshape(8, 128).T)
    identb = np.eye(128, dtype=np.float32).astype(ml_dtypes.bfloat16)
    cstv = np.broadcast_to(np.array([[LN2EPS, LNEPS, EPS, -0.5, -1.0, 0.25, 1.0, 0.0]],
                                    dtype=np.float32), (128, 8)).copy()

    if _NC_CACHE is None:
        _NC_CACHE = build_nc()
    nc = _NC_CACHE

    in_maps = []
    for c in range(8):
        b, half = c // 2, c % 2
        xb = X[b] if half == 0 else np.roll(X[b], -TOWN, axis=0)
        xb = np.ascontiguousarray(xb)
        in_maps.append({
            "x": xb, "xT": np.ascontiguousarray(xb[0:TOWN].T),
            "w_in": W_in, "b_in_t": bin_t, "w_out": W_out, "b_out_t": bout_t,
            "w_ffn": W_ffn, "identb": identb, "cst": cstv,
        })
    res = run_bass_kernel_spmd(nc, in_maps, core_ids=list(range(8)))
    out = np.empty((B, T, D), dtype=np.float32)
    for c in range(8):
        b, half = c // 2, c % 2
        out[b, half * TOWN:(half + 1) * TOWN, :] = res.results[c]["y"].T
    return out
